# version 15 (final)
# speedup vs baseline: 2.1451x; 1.0133x over previous
import numpy as np
import concourse.bass as bass
import concourse.mybir as mybir
from concourse.bass_utils import run_bass_kernel_spmd
from contextlib import ExitStack

F32 = mybir.dt.float32
BF16 = mybir.dt.bfloat16
AF = mybir.ActivationFunctionType
ALU = mybir.AluOpType
AX = mybir.AxisListType

T = 2048
D = 1024
DI = 2048
NS = 16
GD = D // 128
G = DI // 128
GF = 32
NE = 9
NQ = 4
TC = 128
S = 4
NSUB = TC // S
NT = T // 128
NCHUNK = T // TC
EXP_M20 = float(np.exp(-20.0))

_CACHE = {}


class Sem:
    def __init__(self, h):
        self.h = h
        self.v = 0


def build_program():
    nc = bass.Bass("TRN2", target_bir_lowering=False, debug=False)

    x_in = nc.dram_tensor("x_in", [T, D], F32, kind="ExternalInput").ap()
    w_inT = nc.dram_tensor("w_inT", [D, 2 * DI], F32, kind="ExternalInput").ap()
    conv_w = nc.dram_tensor("conv_w", [128, G * 4], F32, kind="ExternalInput").ap()
    conv_b = nc.dram_tensor("conv_b", [128, G], F32, kind="ExternalInput").ap()
    w_xT = nc.dram_tensor("w_xT", [128, G * 33], F32, kind="ExternalInput").ap()
    w_dt = nc.dram_tensor("w_dt", [128, G], F32, kind="ExternalInput").ap()
    b_dt = nc.dram_tensor("b_dt", [128, G], F32, kind="ExternalInput").ap()
    a_log = nc.dram_tensor("a_log", [1, NS], F32, kind="ExternalInput").ap()
    d_par = nc.dram_tensor("d_par", [128, G], F32, kind="ExternalInput").ap()
    w_outT = nc.dram_tensor("w_outT", [DI, D], F32, kind="ExternalInput").ap()
    ln_g = nc.dram_tensor("ln_g", [128, G], F32, kind="ExternalInput").ap()
    ln_b = nc.dram_tensor("ln_b", [128, G], F32, kind="ExternalInput").ap()
    gate_wT = nc.dram_tensor("gate_wT", [D, 8], F32, kind="ExternalInput").ap()
    w1_all = nc.dram_tensor("w1_all", [NE, D, 4096], F32, kind="ExternalInput").ap()
    b1_all = nc.dram_tensor("b1_all", [128, NE * GF], F32, kind="ExternalInput").ap()
    w2_all = nc.dram_tensor("w2_all", [NE, 4096, D], F32, kind="ExternalInput").ap()
    b2_ext = nc.dram_tensor("b2_ext", [NE, D], F32, kind="ExternalInput").ap()
    y_out = nc.dram_tensor("y_out", [T, D], F32, kind="ExternalOutput").ap()

    xn_scr = nc.dram_tensor("xn_scr", [D, T], BF16, kind="Internal").ap()
    winT_scr = nc.dram_tensor("winT_scr", [D, 2 * DI], BF16, kind="Internal").ap()
    woutT_scr = nc.dram_tensor("woutT_scr", [DI, D], BF16, kind="Internal").ap()
    g_scr = nc.dram_tensor("g_scr", [T, D], F32, kind="Internal").ap()
    g2_scr = nc.dram_tensor("g2_scr", [T, D], F32, kind="Internal").ap()
    bc_dt = nc.dram_tensor("bc_dt", [NCHUNK, TC], F32, kind="Internal").ap()
    bc_B = nc.dram_tensor("bc_B", [NCHUNK * NS, TC], F32, kind="Internal").ap()
    bc_C = nc.dram_tensor("bc_C", [NCHUNK * NS, TC], F32, kind="Internal").ap()
    w1_scr = nc.dram_tensor("w1_scr", [NE, D, 4096], BF16, kind="Internal").ap()
    w2_scr = nc.dram_tensor("w2_scr", [NE, 4096, D], BF16, kind="Internal").ap()

    ident_d = nc.inline_tensor(np.eye(128, dtype=np.float32), "identc").ap()
    ones_c_d = nc.inline_tensor(np.ones((128, 1), dtype=np.float32), "onescc").ap()
    ones_r_d = nc.inline_tensor(np.ones((1, 128), dtype=np.float32), "onesrc").ap()

    es_root = ExitStack()

    def SB(stk, name, shape, dt=F32):
        return stk.enter_context(nc.sbuf_tensor(name, shape, dt))

    def PS(stk, name, shape):
        return stk.enter_context(nc.psum_tensor(name, shape, F32))

    R = es_root
    cw = SB(R, "cw", [128, G * 4])
    cbv = SB(R, "cbv", [128, G])
    wdt = SB(R, "wdt", [128, G])
    bdt = SB(R, "bdt", [128, G])
    dpar = SB(R, "dpar", [128, G])
    lng = SB(R, "lng", [128, G])
    lnb = SB(R, "lnb", [128, G])
    ab = SB(R, "ab", [128, NS])
    wxt = SB(R, "wxt", [128, G * 33], BF16)
    b1c = SB(R, "b1c", [128, NE * GF])
    ident = SB(R, "ident", [128, 128])
    ones_c = SB(R, "ones_c", [128, 1])
    ones_r = SB(R, "ones_r", [1, 128])
    comb = SB(R, "comb", [128, NT * NE])

    CONSTS = [1e-5, -0.5, 1.0 / D, 1.0 / DI]
    const_np = np.array([CONSTS], dtype=np.float32)
    const_d = nc.inline_tensor(const_np, "constv").ap()
    const_sb = SB(R, "const_sb", [128, len(CONSTS)])
    for _i, _v in enumerate(CONSTS):
        nc.const_aps.register(F32, float(_v), const_sb[:, _i:_i + 1])

    s_ld = Sem(es_root.enter_context(nc.semaphore("s_ld")))
    s_sc = Sem(es_root.enter_context(nc.semaphore("s_sc")))
    s_gp = Sem(es_root.enter_context(nc.semaphore("s_gp")))
    s_v = Sem(es_root.enter_context(nc.semaphore("s_v")))
    s_a = Sem(es_root.enter_context(nc.semaphore("s_a")))
    s_t = Sem(es_root.enter_context(nc.semaphore("s_t")))
    s_p = Sem(es_root.enter_context(nc.semaphore("s_p")))
    s_ch = Sem(es_root.enter_context(nc.semaphore("s_ch")))

    ENGS = ("sync", "vector", "scalar", "tensor", "gpsimd")

    def new_sched():
        return {k: [] for k in ENGS}

    def run_block(sched, name):
        with nc.Block(name) as block:
            @block.sync
            def _(eng):
                for fn in sched["sync"]:
                    fn(eng)

            @block.vector
            def _(eng):
                for fn in sched["vector"]:
                    fn(eng)

            @block.scalar
            def _(eng):
                for fn in sched["scalar"]:
                    fn(eng)

            @block.tensor
            def _(eng):
                for fn in sched["tensor"]:
                    fn(eng)

            @block.gpsimd
            def _(eng):
                for fn in sched["gpsimd"]:
                    fn(eng)

    es01 = ExitStack()
    combf = SB(es01, "combf", [NS, T])
    b2e = SB(es01, "b2e", [NE, D])

    es0 = ExitStack()
    s_x = Sem(es0.enter_context(nc.semaphore("s_x")))
    xt0 = SB(es0, "xt0", [128, D])
    xc0 = SB(es0, "xc0", [128, D])
    sq0 = SB(es0, "sq0", [128, D])
    cols = SB(es0, "cols", [128, 8])
    xnf = [SB(es0, f"xnf{i}", [128, 128]) for i in range(2)]
    xnb = SB(es0, "xnb", [128, GD * 128], BF16)
    probs = SB(es0, "probs", [128, 8])
    s8t = SB(es0, "s8t", [128, 8])
    ctile = SB(es0, "ctile", [128, 16])
    wstage = SB(es0, "wstage", [128, 4096])
    wbf = SB(es0, "wbf", [128, 4096], BF16)
    gwt = SB(es0, "gwt", [128, GD * 8])
    ps_tp = PS(es0, "ps_tp", [128, 128])
    ps_g = PS(es0, "ps_g", [128, 8])
    ps_cf = PS(es0, "ps_cf", [16, 128])

    SCH = new_sched()

    def E(eng, fn):
        SCH[eng].append(fn)

    def W(eng, sem, extra=0):
        v = sem.v + extra
        E(eng, lambda e, h=sem.h, v=v: e.wait_ge(h, v))

    def p0_loads(sync):
        for dst, src in ((cw, conv_w), (cbv, conv_b), (wdt, w_dt), (bdt, b_dt),
                         (dpar, d_par), (lng, ln_g), (lnb, ln_b), (b1c, b1_all),
                         (b2e, b2_ext), (ident, ident_d), (ones_c, ones_c_d),
                         (ones_r, ones_r_d)):
            sync.dma_start(out=dst[:], in_=src[:]).then_inc(s_ld.h, 16)
        sync.dma_start(out=ab[:], in_=a_log.broadcast_to((128, NS))).then_inc(s_ld.h, 16)
        sync.dma_start(out=const_sb[:], in_=const_d.broadcast_to((128, const_np.shape[1]))).then_inc(s_ld.h, 16)
        sync.dma_start(out=wstage[:, 0:G * 33], in_=w_xT[:]).then_inc(s_ld.h, 16)
        for c in range(GD):
            sync.dma_start(out=gwt[:, c * 8:(c + 1) * 8],
                           in_=gate_wT[c * 128:(c + 1) * 128, :]).then_inc(s_ld.h, 16)
    s_ld.v += 16 * (15 + GD)
    E("sync", p0_loads)

    W("vector", s_ld)

    def p0_wx(vector):
        vector.tensor_copy(wxt[:], wstage[:, 0:G * 33])
        vector.tensor_scalar(ab[:], ab[:], 5.0, None, ALU.min).then_inc(s_v.h, 1)
    s_v.v += 1
    E("vector", p0_wx)
    W("scalar", s_v)

    def p0_a(scalar, _w=s_a.v + 1):
        scalar.activation(ab[:], ab[:], AF.Exp).then_inc(s_a.h, 1)
        scalar.wait_ge(s_a.h, _w)
        scalar.mul(ab[:], ab[:], -1.0).then_inc(s_a.h, 1)
    s_a.v += 2
    E("scalar", p0_a)

    for tt in range(NT):
        if tt > 0:
            W("sync", s_t)
        def ldx(sync, tt=tt):
            sync.dma_start(out=xt0[:], in_=x_in[tt * 128:(tt + 1) * 128, :]).then_inc(s_x.h, 16)
        s_x.v += 16
        E("sync", ldx)

        W("vector", s_x)
        if tt == 0:
            W("vector", s_ld)
        def ln1(vector, _w=s_v.v):
            vector.reduce_sum(cols[:, 0:1], xt0[:], AX.X).then_inc(s_v.h, 1)
            vector.wait_ge(s_v.h, _w + 1)
            vector.tensor_scalar(cols[:, 0:1], cols[:, 0:1], 1.0 / D, None, ALU.mult).then_inc(s_v.h, 1)
            vector.wait_ge(s_v.h, _w + 2)
            vector.tensor_single_scalar(xc0[:], xt0[:], cols[:, 0:1], ALU.subtract).then_inc(s_v.h, 1)
        s_v.v += 3
        E("vector", ln1)
        W("scalar", s_v)
        def ln2(scalar, _w=s_a.v):
            scalar.activation(sq0[:], xc0[:], AF.Square, accum_out=cols[:, 1:2]).then_inc(s_a.h, 1)
            scalar.wait_ge(s_a.h, _w + 1)
            scalar.activation(cols[:, 2:3], cols[:, 1:2], AF.Ln, scale=1.0 / D, bias=1e-5).then_inc(s_a.h, 1)
            scalar.wait_ge(s_a.h, _w + 2)
            scalar.activation(cols[:, 3:4], cols[:, 2:3], AF.Exp, scale=-0.5).then_inc(s_a.h, 1)
        s_a.v += 3
        E("scalar", ln2)
        W("vector", s_a)
        def ln3(vector):
            vector.tensor_single_scalar(xt0[:], xc0[:], cols[:, 3:4], ALU.mult).then_inc(s_v.h, 1)
        s_v.v += 1
        E("vector", ln3)

        for c in range(GD):
            if c == 0:
                W("tensor", s_v)
                W("tensor", s_a)
            else:
                W("tensor", s_a)
            def tp(tensor, c=c):
                tensor.transpose(ps_tp[:], xt0[:, c * 128:(c + 1) * 128], ident[:]).then_inc(s_t.h, 1)
            s_t.v += 1
            E("tensor", tp)
            W("scalar", s_t)
            if c == 0 and tt > 0:
                W("scalar", s_sc)
            def evac(scalar, c=c):
                scalar.copy(xnf[c % 2][:], ps_tp[:])
                scalar.activation(xnb[:, c * 128:(c + 1) * 128], ps_tp[:], AF.Copy).then_inc(s_a.h, 1)
            s_a.v += 1
            E("scalar", evac)
            W("tensor", s_a)
            def gmm(tensor, c=c):
                tensor.matmul(ps_g[:], xnf[c % 2][:], gwt[:, c * 8:(c + 1) * 8],
                              start=(c == 0), stop=(c == GD - 1)).then_inc(s_t.h, 1)
            s_t.v += 1
            E("tensor", gmm)

        W("scalar", s_a)
        def st_xn(scalar, tt=tt):
            for c in range(GD):
                scalar.dma_start(out=xn_scr[c * 128:(c + 1) * 128, tt * 128:(tt + 1) * 128],
                                 in_=xnb[:, c * 128:(c + 1) * 128]).then_inc(s_sc.h, 16)
        s_sc.v += 16 * GD
        E("scalar", st_xn)

        W("vector", s_t)
        def sm1(vector, _w=s_v.v):
            vector.reduce_max(cols[:, 4:5], ps_g[:], AX.X).then_inc(s_v.h, 1)
            vector.wait_ge(s_v.h, _w + 1)
            vector.tensor_single_scalar(probs[:], ps_g[:], cols[:, 4:5], ALU.subtract).then_inc(s_v.h, 1)
        s_v.v += 2
        E("vector", sm1)
        W("scalar", s_v)
        def sm2(scalar):
            scalar.activation(probs[:], probs[:], AF.Exp).then_inc(s_a.h, 1)
        s_a.v += 1
        E("scalar", sm2)
        W("vector", s_a)
        def sm3(vector, tt=tt, _w=s_v.v):
            ops = [
                lambda: vector.reduce_sum(cols[:, 5:6], probs[:], AX.X),
                lambda: vector.reciprocal(cols[:, 6:7], cols[:, 5:6]),
                lambda: vector.tensor_single_scalar(probs[:], probs[:], cols[:, 6:7], ALU.mult),
                lambda: vector.max(s8t[:], probs[:]),
                lambda: vector.tensor_add(cols[:, 7:8], s8t[:, 0:1], s8t[:, 1:2]),
                lambda: vector.reciprocal(cols[:, 7:8], cols[:, 7:8]),
                lambda: vector.tensor_single_scalar(ctile[:, 0:8], probs[:], s8t[:, 1:2], ALU.is_ge),
                lambda: vector.tensor_tensor(ctile[:, 0:8], ctile[:, 0:8], probs[:], ALU.mult),
                lambda: vector.tensor_single_scalar(ctile[:, 0:8], ctile[:, 0:8], cols[:, 7:8], ALU.mult),
                lambda: vector.memset(ctile[:, 8:9], 1.0),
                lambda: vector.memset(ctile[:, 9:16], 0.0),
                lambda: vector.tensor_copy(comb[:, tt * NE:(tt + 1) * NE], ctile[:, 0:NE]),
            ]
            for j, f in enumerate(ops):
                f().then_inc(s_v.h, 1)
                vector.wait_ge(s_v.h, _w + j + 1)
        s_v.v += 12
        E("vector", sm3)
        W("tensor", s_v)
        W("tensor", s_a)
        def ctp(tensor):
            tensor.transpose(ps_cf[:], ctile[:], ident[:]).then_inc(s_t.h, 1)
        s_t.v += 1
        E("tensor", ctp)
        W("scalar", s_t)
        def cfe(scalar, tt=tt):
            scalar.copy(combf[0:NS, tt * 128:(tt + 1) * 128], ps_cf[:]).then_inc(s_a.h, 1)
        s_a.v += 1
        E("scalar", cfe)

    v_after_p0_sc = s_sc.v
    W("sync", s_ld)
    W("scalar", s_sc)
    run_block(SCH, "p0")
    es0.close()

    es1 = ExitStack()
    w1b = SB(es1, "w1b", [128, 8 * 1024], BF16)
    w2b = SB(es1, "w2b", [128, 8 * 1024], BF16)
    xng = SB(es1, "xng", [128, 8 * 512], BF16)
    h1b = SB(es1, "h1b", [128, 8 * 512], BF16)
    gst = [SB(es1, f"gst{i}", [128, D]) for i in range(2)]
    ps_h1 = [PS(es1, f"ps_h1{i}", [128, 512]) for i in range(2)]
    ps_w2 = [PS(es1, f"ps_w2{i}", [128, 1024]) for i in range(2)]

    SCH = new_sched()

    def E(eng, fn):
        SCH[eng].append(fn)

    def W(eng, sem, extra=0):
        v = sem.v + extra
        E(eng, lambda e, h=sem.h, v=v: e.wait_ge(h, v))

    s_xn2 = [Sem(es1.enter_context(nc.semaphore(f"s_xn{i}"))) for i in range(2)]
    tile_sem = [Sem(es1.enter_context(nc.semaphore(f"s_tile{t}"))) for t in range(NT)]
    s_ldg = [Sem(es1.enter_context(nc.semaphore(f"s_ldg{i}"))) for i in range(2)]
    gacc_tile = {}

    xn_ld = {}

    def sched_ldxn(gi, guard):
        if guard is not None:
            W("sync", s_t, guard - s_t.v)
        def ldxn(sync, g=(gi % 4), b=(gi % 2)):
            for k in range(GD):
                sync.dma_start(out=xng4[b][:, k * 512:(k + 1) * 512],
                               in_=xn_scr[k * 128:(k + 1) * 128,
                                          g * 512:(g + 1) * 512]).then_inc(s_xn2[gi % 2].h, 16)
        s_xn2[gi % 2].v += 16 * GD
        E("sync", ldxn)
        xn_ld[gi] = s_xn2[gi % 2].v

    W("sync", s_sc, v_after_p0_sc - s_sc.v)
    sched_ldxn(0, None)
    sched_ldxn(1, None)


    jobs = []
    for i in range(GD):
        for h in range(4):
            jobs.append((w_inT[i * 128:(i + 1) * 128, h * 1024:(h + 1) * 1024],
                         winT_scr[i * 128:(i + 1) * 128, h * 1024:(h + 1) * 1024], D))
    for j in range(G):
        jobs.append((w_outT[j * 128:(j + 1) * 128, :], woutT_scr[j * 128:(j + 1) * 128, :], D))
    wstage2 = [SB(es1, "wstageA", [128, 1024]), SB(es1, "wstageB", [128, 1024])]
    wbf2 = [SB(es1, "wbfA", [128, 1024], BF16), SB(es1, "wbfB", [128, 1024], BF16)]
    s_pc = [Sem(es1.enter_context(nc.semaphore(f"s_pc{i}"))) for i in range(2)]
    s_pst = [Sem(es1.enter_context(nc.semaphore(f"s_pst{i}"))) for i in range(2)]
    cast_v = {}
    for i, (sap, dap, width) in enumerate(jobs):
        par = i % 2
        if i >= 2:
            W("sync", s_v, cast_v[i - 2] - s_v.v)
        def ldw(sync, sap=sap, width=width, par=par):
            sync.dma_start(out=wstage2[par][:, 0:width], in_=sap).then_inc(s_pc[par].h, 16)
        s_pc[par].v += 16
        E("sync", ldw)
        W("vector", s_pc[par])
        if i >= 2:
            W("vector", s_pst[par])
        def cst(vector, width=width, par=par):
            vector.tensor_copy(wbf2[par][:, 0:width], wstage2[par][:, 0:width]).then_inc(s_v.h, 1)
        s_v.v += 1
        cast_v[i] = s_v.v
        E("vector", cst)
        W("scalar", s_v)
        def stw(scalar, dap=dap, width=width, par=par):
            scalar.dma_start(out=dap, in_=wbf2[par][:, 0:width]).then_inc(s_pst[par].h, 16)
        s_pst[par].v += 16
        E("scalar", stw)


    n_acc = 0
    NEQ = NE * NQ
    s_wl = Sem(es1.enter_context(nc.semaphore("s_wl")))
    stg1 = [SB(es1, "stg1", [128, 8 * 1024])] * 2
    stg2 = [SB(es1, "stg2", [128, 8 * 1024])] * 2
    w1b = [w1b, SB(es1, "w1bB", [128, 8 * 1024], BF16)]
    w2b = [w2b, SB(es1, "w2bB", [128, 8 * 1024], BF16)]
    xng4 = [xng, SB(es1, "xngB", [128, 8 * 512], BF16)]
    wl_v = {}
    cst_v = {}
    mm_end = {}

    def sched_wload(eq):
        e_, q_ = divmod(eq, NQ)
        par = eq % 2
        if eq >= 1:
            W("gpsimd", s_v, cst_v[eq - 1] - s_v.v)
        def ldw(gpsimd, e=e_, q=q_, par=par):
            for k in range(GD):
                gpsimd.dma_start(out=stg1[par][:, k * 1024:(k + 1) * 1024],
                                 in_=w1_all[e, k * 128:(k + 1) * 128,
                                            q * 1024:(q + 1) * 1024]).then_inc(s_wl.h, 16)
            for k in range(GD):
                gpsimd.dma_start(out=stg2[par][:, k * 1024:(k + 1) * 1024],
                                 in_=w2_all[e, q * 1024 + k * 128:q * 1024 + (k + 1) * 128,
                                            :]).then_inc(s_wl.h, 16)
        s_wl.v += 32 * GD
        E("gpsimd", ldw)
        wl_v[eq] = s_wl.v

    def sched_wcast(eq):
        par = eq % 2
        W("vector", s_wl, wl_v[eq] - s_wl.v)
        if eq >= 2:
            W("vector", s_t, mm_end[eq - 2] - s_t.v)
        def cst(vector, par=par):
            vector.tensor_copy(w1b[par][:], stg1[par][:])
            vector.tensor_copy(w2b[par][:], stg2[par][:]).then_inc(s_v.h, 1)
        s_v.v += 1
        cst_v[eq] = s_v.v
        E("vector", cst)

    sched_wload(0)
    sched_wcast(0)
    sched_wload(1)
    sched_wcast(1)

    gi = 2
    for eq in range(NEQ):
        e, q = divmod(eq, NQ)
        first = (eq == 0)
        par = eq % 2
        W("tensor", s_v, cst_v[eq] - s_v.v)
        for grp in range(4):
            gg = eq * 4 + grp
            W("tensor", s_xn2[gg % 2], xn_ld[gg] - s_xn2[gg % 2].v)
            for ft in range(8):
                if eq > 0 or grp > 0 or ft >= 2:
                    W("tensor", s_a, -1)
                def h1mm(tensor, ft=ft, par=par, b=(gg % 2)):
                    ins = None
                    for k in range(GD):
                        ins = tensor.matmul(ps_h1[ft % 2][:],
                                            w1b[par][:, k * 1024 + ft * 128:k * 1024 + (ft + 1) * 128],
                                            xng4[b][:, k * 512:(k + 1) * 512],
                                            start=(k == 0), stop=(k == GD - 1))
                    ins.then_inc(s_t.h, 1)
                s_t.v += 1
                E("tensor", h1mm)
                W("scalar", s_t)
                def gelu(scalar, e=e, q=q, ft=ft):
                    idx = e * GF + q * 8 + ft
                    scalar.activation(h1b[:, ft * 512:(ft + 1) * 512], ps_h1[ft % 2][:],
                                      AF.Gelu, bias=b1c[:, idx:idx + 1]).then_inc(s_a.h, 1)
                s_a.v += 1
                E("scalar", gelu)
            if gi < NEQ * 4:
                sched_ldxn(gi, s_t.v)
                gi += 1
            W("tensor", s_a)
            for ts4 in range(4):
                tt = grp * 4 + ts4
                if ts4 >= 2:
                    W("tensor", s_a, -1)
                def w2mm(tensor, ts4=ts4, par=par):
                    ins = None
                    for fc in range(8):
                        for dh in range(2):
                            ins = tensor.matmul(
                                ps_w2[ts4 % 2][:, dh * 512:(dh + 1) * 512],
                                h1b[:, fc * 512 + ts4 * 128:fc * 512 + (ts4 + 1) * 128],
                                w2b[par][:, fc * 1024 + dh * 512:fc * 1024 + (dh + 1) * 512],
                                start=(fc == 0), stop=(fc == 7))
                    ins.then_inc(s_t.h, 1)
                s_t.v += 1
                E("tensor", w2mm)
                W("scalar", s_t)
                if n_acc >= 2:
                    pt, pv = gacc_tile[n_acc - 2]
                    W("scalar", tile_sem[pt], pv - tile_sem[pt].v)
                def cevac(scalar, e=e, tt=tt, b=(n_acc % 2), ts4=ts4):
                    scalar.activation(gst[b][:], ps_w2[ts4 % 2][:], AF.Copy,
                                      scale=comb[:, tt * NE + e:tt * NE + e + 1]).then_inc(s_a.h, 1)
                s_a.v += 1
                E("scalar", cevac)
                W("gpsimd", s_a)
                W("gpsimd", tile_sem[tt])
                def gacc(gpsimd, tt=tt, b=(n_acc % 2), first=first):
                    if first:
                        gpsimd.dma_start(out=g_scr[tt * 128:(tt + 1) * 128, :],
                                         in_=gst[b][:]).then_inc(tile_sem[tt].h, 16)
                    else:
                        gpsimd.dma_start(out=g_scr[tt * 128:(tt + 1) * 128, :], in_=gst[b][:],
                                         accum_op=ALU.add).then_inc(tile_sem[tt].h, 16)
                tile_sem[tt].v += 16
                gacc_tile[n_acc] = (tt, tile_sem[tt].v)
                E("gpsimd", gacc)
                n_acc += 1
        mm_end[eq] = s_t.v
        if eq + 2 < NEQ:
            sched_wload(eq + 2)
            sched_wcast(eq + 2)


    gt1 = [SB(es1, f"gt1{i}", [128, D]) for i in range(2)]
    for _tt in range(NT):
        W("sync", tile_sem[_tt])
    for tt in range(NT):
        if tt >= 2:
            W("sync", s_sc)
        def ldg(sync, tt=tt):
            sync.dma_start(out=gt1[tt % 2][:], in_=g_scr[tt * 128:(tt + 1) * 128, :]).then_inc(s_ldg[tt % 2].h, 16)
        s_ldg[tt % 2].v += 16
        E("sync", ldg)
        if tt == 0:
            W("tensor", s_a)
        else:
            W("tensor", s_v)
        def b2mm(tensor, tt=tt):
            tensor.matmul(ps_h1[0][:], combf[0:NE, tt * 128:(tt + 1) * 128], b2e[:, 0:512],
                          start=True, stop=True)
            tensor.matmul(ps_h1[1][:], combf[0:NE, tt * 128:(tt + 1) * 128], b2e[:, 512:1024],
                          start=True, stop=True).then_inc(s_t.h, 1)
        s_t.v += 1
        E("tensor", b2mm)
        W("vector", s_ldg[tt % 2])
        W("vector", s_t)
        def gadd(vector, tt=tt):
            vector.tensor_tensor(gt1[tt % 2][:, 0:512], gt1[tt % 2][:, 0:512], ps_h1[0][:], ALU.add)
            vector.tensor_tensor(gt1[tt % 2][:, 512:1024], gt1[tt % 2][:, 512:1024],
                                 ps_h1[1][:], ALU.add).then_inc(s_v.h, 1)
        s_v.v += 1
        E("vector", gadd)
        W("scalar", s_v)
        def gsilu(scalar, tt=tt, _w=s_a.v + 1):
            scalar.activation(gt1[tt % 2][:], gt1[tt % 2][:], AF.Silu).then_inc(s_a.h, 1)
            scalar.wait_ge(s_a.h, _w)
            scalar.dma_start(out=g2_scr[tt * 128:(tt + 1) * 128, :],
                             in_=gt1[tt % 2][:]).then_inc(s_sc.h, 16)
        s_sc.v += 16
        s_a.v += 1
        E("scalar", gsilu)
    W("scalar", s_sc)
    W("scalar", s_pst[0])
    W("scalar", s_pst[1])
    W("sync", s_ld)
    run_block(SCH, "p1")
    es1.close()
    es01.close()

    es2 = ExitStack()
    xnc = SB(es2, "xnc", [128, GD * 256], BF16)
    winb = [SB(es2, f"winb{i}", [128, 8 * 512], BF16) for i in range(2)]
    xcb = SB(es2, "xcb", [128, G * 259])
    zsb = SB(es2, "zsb", [128, G * 256])
    ub = SB(es2, "ub", [128, G * 256])
    ubf = SB(es2, "ubf", [128, G * 256], BF16)
    scr16 = SB(es2, "scr16", [128, 4096])
    dtb = SB(es2, "dtb", [128, 256])
    dtt = SB(es2, "dtt", [128, G * TC])
    dut = SB(es2, "dut", [128, G * TC])
    bb = SB(es2, "bb", [128, NS * TC])
    cbb = SB(es2, "cbb", [128, NS * TC])
    dAb = [SB(es2, f"dAb{i}", [128, S * 256]) for i in range(2)]
    dBb = [SB(es2, f"dBb{i}", [128, S * 256]) for i in range(2)]
    Hb = [SB(es2, f"Hb{i}", [128, S * 256]) for i in range(2)]
    ustate = SB(es2, "ustate", [128, 256])
    stept = SB(es2, "stept", [128, 256])
    yb = SB(es2, "yb", [128, G * 256])
    ygbf = SB(es2, "ygbf", [128, G * 256], BF16)
    wx_sb = SB(es2, "wx_sb", [33, 256])
    lnrow = SB(es2, "lnrow", [1, 256])
    rowb = SB(es2, "rowb", [128, 256])
    woutb = [SB(es2, f"woutb{i}", [128, 1024], BF16) for i in range(2)]
    stile = SB(es2, "stile", [128, D])
    g2t = SB(es2, "g2t", [128, D])
    xt2 = SB(es2, "xt2", [128, D])
    s_win = [Sem(es2.enter_context(nc.semaphore(f"s_win{i}"))) for i in range(2)]
    s_wo = [Sem(es2.enter_context(nc.semaphore(f"s_wo{i}"))) for i in range(2)]
    s_fin = Sem(es2.enter_context(nc.semaphore("s_fin")))
    ps_xz = [PS(es2, f"ps_xz{i}", [128, 256]) for i in range(2)]
    ps_wx = PS(es2, "ps_wx", [33, 256])
    ps_row = PS(es2, "ps_row", [1, 256])
    ps_bc = PS(es2, "ps_bc", [128, 256])
    ps_s = PS(es2, "ps_s", [128, 1024])

    SCH = new_sched()

    def E(eng, fn):
        SCH[eng].append(fn)

    def W(eng, sem, extra=0):
        v = sem.v + extra
        E(eng, lambda e, h=sem.h, v=v: e.wait_ge(h, v))

    gsub = 0
    n_out = 0
    for cc in range(NCHUNK // 2):
        c0, c1 = 2 * cc, 2 * cc + 1
        def tailcpy(vector, cc=cc):
            x3 = xcb[:].rearrange("p (g t) -> p g t", g=G)
            if cc == 0:
                vector.memset(x3[:, :, 0:3], 0.0).then_inc(s_v.h, 1)
            else:
                vector.tensor_copy(x3[:, :, 0:3], x3[:, :, 256:259]).then_inc(s_v.h, 1)
        s_v.v += 1
        E("vector", tailcpy)
        v_tail = s_v.v

        W("sync", s_t)
        def ldxn2(sync, cc=cc):
            for k in range(GD):
                sync.dma_start(out=xnc[:, k * 256:(k + 1) * 256],
                               in_=xn_scr[k * 128:(k + 1) * 128,
                                          cc * 256:(cc + 1) * 256]).then_inc(s_ld.h, 16)
        s_ld.v += 16 * GD
        E("sync", ldxn2)

        for etg in range(8):
            if cc > 0 or etg >= 2:
                W("sync", s_t)
            def ldwin(sync, etg=etg):
                for k in range(GD):
                    sync.dma_start(out=winb[etg % 2][:, k * 512:(k + 1) * 512],
                                   in_=winT_scr[k * 128:(k + 1) * 128,
                                                etg * 512:(etg + 1) * 512]).then_inc(s_win[etg % 2].h, 16)
            s_win[etg % 2].v += 16 * GD
            E("sync", ldwin)
            W("tensor", s_win[etg % 2])
            W("tensor", s_ld)
            for et4 in range(4):
                et = etg * 4 + et4
                if et >= 2 or cc > 0:
                    W("tensor", s_a, -1)
                def xzmm(tensor, etg=etg, et4=et4, et=et):
                    ins = None
                    for k in range(GD):
                        ins = tensor.matmul(ps_xz[et % 2][:],
                                            winb[etg % 2][:, k * 512 + et4 * 128:k * 512 + (et4 + 1) * 128],
                                            xnc[:, k * 256:(k + 1) * 256],
                                            start=(k == 0), stop=(k == GD - 1))
                    ins.then_inc(s_t.h, 1)
                s_t.v += 1
                E("tensor", xzmm)
                W("scalar", s_t)
                if et == 0:
                    W("scalar", s_v, v_tail - s_v.v)
                def xzevac(scalar, et=et):
                    if et < G:
                        scalar.activation(xcb[:, et * 259 + 3:et * 259 + 259], ps_xz[et % 2][:],
                                          AF.Copy).then_inc(s_a.h, 1)
                    else:
                        scalar.activation(zsb[:, (et - G) * 256:(et - G + 1) * 256], ps_xz[et % 2][:],
                                          AF.Silu).then_inc(s_a.h, 1)
                s_a.v += 1
                E("scalar", xzevac)

        W("vector", s_a)
        def convf(vector):
            x3 = xcb[:].rearrange("p (g t) -> p g t", g=G)
            u3 = ub[:].rearrange("p (g t) -> p g t", g=G)
            t3 = scr16[:].rearrange("p (g t) -> p g t", g=G)
            cw3 = cw[:].rearrange("p (g k) -> p g k", g=G)
            w0 = cw3[:, :, 0:1].broadcast_to((128, G, 256))
            vector.tensor_tensor(u3, x3[:, :, 0:256], w0, ALU.mult)
            for k in range(1, 4):
                wk = cw3[:, :, k:k + 1].broadcast_to((128, G, 256))
                vector.tensor_tensor(t3, x3[:, :, k:k + 256], wk, ALU.mult)
                vector.tensor_tensor(u3, u3, t3, ALU.add)
            cbv_v = cbv[:].unsqueeze(2).broadcast_to((128, G, 256))
            vector.tensor_tensor(u3, u3, cbv_v, ALU.add).then_inc(s_v.h, 1)
        s_v.v += 1
        E("vector", convf)
        W("scalar", s_v)
        def usilu(scalar):
            scalar.activation(ub[:], ub[:], AF.Silu).then_inc(s_a.h, 1)
        s_a.v += 1
        E("scalar", usilu)
        W("vector", s_a)
        def ucast(vector):
            vector.tensor_copy(ubf[:], ub[:]).then_inc(s_v.h, 1)
        s_v.v += 1
        E("vector", ucast)

        W("tensor", s_v)
        def wxmm(tensor):
            ins = None
            for g in range(G):
                ins = tensor.matmul(ps_wx[:], wxt[:, g * 33:(g + 1) * 33],
                                    ubf[:, g * 256:(g + 1) * 256],
                                    start=(g == 0), stop=(g == G - 1))
            ins.then_inc(s_t.h, 1)
        s_t.v += 1
        E("tensor", wxmm)
        W("scalar", s_t)
        W("scalar", s_sc)
        def wxevac(scalar, c0=c0, c1=c1, _w=s_a.v + 1):
            scalar.copy(wx_sb[:], ps_wx[:]).then_inc(s_a.h, 1)
            scalar.wait_ge(s_a.h, _w)
            scalar.dma_start(out=bc_dt[c0:c0 + 1, :], in_=wx_sb[0:1, 0:128]).then_inc(s_sc.h, 16)
            scalar.dma_start(out=bc_dt[c1:c1 + 1, :], in_=wx_sb[0:1, 128:256]).then_inc(s_sc.h, 16)
            scalar.dma_start(out=bc_B[c0 * NS:(c0 + 1) * NS, :], in_=wx_sb[1:17, 0:128]).then_inc(s_sc.h, 16)
            scalar.dma_start(out=bc_B[c1 * NS:(c1 + 1) * NS, :], in_=wx_sb[1:17, 128:256]).then_inc(s_sc.h, 16)
            scalar.dma_start(out=bc_C[c0 * NS:(c0 + 1) * NS, :], in_=wx_sb[17:33, 0:128]).then_inc(s_sc.h, 16)
            scalar.dma_start(out=bc_C[c1 * NS:(c1 + 1) * NS, :], in_=wx_sb[17:33, 128:256]).then_inc(s_sc.h, 16)
        s_sc.v += 96
        s_a.v += 1
        E("scalar", wxevac)

        for ci, c in enumerate((c0, c1)):
            toff = ci * TC
            W("sync", s_sc)
            if gsub > 0:
                W("sync", s_ch)
            def ldbc(sync, c=c, ci=ci):
                sync.dma_start(out=dtb[:, ci * TC:(ci + 1) * TC],
                               in_=bc_dt[c:c + 1, :].broadcast_to((128, TC))).then_inc(s_ld.h, 16)
                sync.dma_start(out=bb[:], in_=bc_B[c * NS:(c + 1) * NS, :].flatten()
                               .unsqueeze(0).broadcast_to((128, NS * TC))).then_inc(s_ld.h, 16)
                sync.dma_start(out=cbb[:], in_=bc_C[c * NS:(c + 1) * NS, :].flatten()
                               .unsqueeze(0).broadcast_to((128, NS * TC))).then_inc(s_ld.h, 16)
            s_ld.v += 48
            E("sync", ldbc)
            W("vector", s_ld)
            def dt1(vector, ci=ci):
                d3 = dtt[:].rearrange("p (g t) -> p g t", g=G)
                dtb_v = dtb[:, ci * TC:(ci + 1) * TC].unsqueeze(1).broadcast_to((128, G, TC))
                wdt_v = wdt[:].unsqueeze(2).broadcast_to((128, G, TC))
                bdt_v = bdt[:].unsqueeze(2).broadcast_to((128, G, TC))
                vector.tensor_tensor(d3, dtb_v, wdt_v, ALU.mult)
                vector.tensor_tensor(d3, d3, bdt_v, ALU.add).then_inc(s_v.h, 1)
            s_v.v += 1
            E("vector", dt1)
            W("scalar", s_v)
            def dt2(scalar):
                scalar.activation(scr16[:, 0:G * TC], dtt[:], AF.Abs)
                scalar.activation(scr16[:, 0:G * TC], scr16[:, 0:G * TC], AF.Exp, scale=-1.0)
                scalar.activation(scr16[:, 0:G * TC], scr16[:, 0:G * TC], AF.Ln, bias=1.0).then_inc(s_a.h, 1)
            s_a.v += 1
            E("scalar", dt2)
            W("vector", s_a)
            def dt3(vector, ci=ci):
                vector.tensor_scalar(dtt[:], dtt[:], 0.0, None, ALU.max)
                vector.tensor_tensor(dtt[:], dtt[:], scr16[:, 0:G * TC], ALU.add)
                vector.tensor_scalar(dtt[:], dtt[:], 10.0, 1e-4, ALU.min, ALU.max)
                u3s = ub[:].rearrange("p (g t) -> p g t", g=G)[:, :, ci * TC:(ci + 1) * TC]
                d3 = dut[:].rearrange("p (g t) -> p g t", g=G)
                dt3_ = dtt[:].rearrange("p (g t) -> p g t", g=G)
                vector.tensor_tensor(d3, dt3_, u3s, ALU.mult).then_inc(s_v.h, 1)
            s_v.v += 1
            E("vector", dt3)
            v_dt = s_v.v

            for i in range(NSUB):
                buf = gsub % 2
                t0 = i * S
                W("scalar", s_v, v_dt - s_v.v)
                if gsub >= 2:
                    W("scalar", s_ch, gsub - 1 - s_ch.v)
                def dexp(scalar, t0=t0, buf=buf):
                    dA4 = dAb[buf][:].rearrange("p (s g n) -> p s g n", s=S, g=G)
                    dt_v = dtt[:].rearrange("p (g t) -> p g t", g=G)[:, :, t0:t0 + S] \
                        .transpose([0, 2, 1])
                    ins = None
                    for n in range(NS):
                        ins = scalar.activation(dA4[:, :, :, n], dt_v, AF.Exp, scale=ab[:, n:n + 1])
                    ins.then_inc(s_a.h, 1)
                s_a.v += 1
                E("scalar", dexp)
                W("vector", s_a)
                def chain(vector, buf=buf, t0=t0, gsub=gsub, toff=toff):
                    du_v = dut[:].rearrange("p (g t) -> p g t", g=G)[:, :, t0:t0 + S] \
                        .transpose([0, 2, 1]).unsqueeze(3).broadcast_to((128, S, G, NS))
                    bb_v = bb[:].rearrange("p (n t) -> p n t", n=NS)[:, :, t0:t0 + S] \
                        .transpose([0, 2, 1]).unsqueeze(2).broadcast_to((128, S, G, NS))
                    dB4 = dBb[0][:].rearrange("p (s g n) -> p s g n", s=S, g=G)
                    vector.tensor_tensor(dB4, du_v, bb_v, ALU.mult)
                    vector.tensor_scalar(dBb[0][:], dBb[0][:], 10.0, -10.0, ALU.min, ALU.max)
                    if gsub == 0:
                        vector.memset(Hb[1][:, (S - 1) * 256:S * 256], 0.0)
                        vector.memset(ustate[:], 0.0)
                    hprev = Hb[1 - buf][:, (S - 1) * 256:S * 256]
                    for s in range(S):
                        hslot = Hb[buf][:, s * 256:(s + 1) * 256]
                        vector.scalar_tensor_tensor(stept[:], dAb[buf][:, s * 256:(s + 1) * 256], EXP_M20,
                                                    hprev, ALU.max, ALU.mult)
                        vector.scalar_tensor_tensor(ustate[:], dBb[0][:, s * 256:(s + 1) * 256], -10.0,
                                                    stept[:], ALU.max, ALU.add)
                        vector.tensor_scalar(hslot, ustate[:], 100.0, -100.0, ALU.min, ALU.max)
                        hprev = hslot
                    cb_v = cbb[:].rearrange("p (n t) -> p n t", n=NS)[:, :, t0:t0 + S] \
                        .transpose([0, 2, 1]).unsqueeze(2).broadcast_to((128, S, G, NS))
                    h4 = Hb[buf][:].rearrange("p (s g n) -> p s g n", s=S, g=G)
                    yt4 = scr16[:, 0:S * 256].rearrange("p (s g n) -> p s g n", s=S, g=G)
                    vector.tensor_tensor(yt4, h4, cb_v, ALU.mult)
                    yo_v = yb[:].rearrange("p (g t) -> p g t", g=G)[:, :, toff + t0:toff + t0 + S] \
                        .transpose([0, 2, 1])
                    vector.tensor_reduce(yo_v, yt4, AX.X, ALU.add).then_inc(s_ch.h, 1)
                    vector.engine_nop().then_inc(s_v.h, 1)
                s_v.v += 1
                s_ch.v += 1
                E("vector", chain)
                gsub += 1

        def red_gsum(vector):
            y3 = yb[:].rearrange("p (g t) -> p g t", g=G).transpose([0, 2, 1])
            vector.tensor_reduce(rowb[:], y3, AX.X, ALU.add).then_inc(s_v.h, 1)
        s_v.v += 1
        E("vector", red_gsum)
        W("tensor", s_v)
        def mm_psum(tensor):
            tensor.matmul(ps_row[:], ones_c[:], rowb[:], start=True, stop=True).then_inc(s_t.h, 1)
        s_t.v += 1
        E("tensor", mm_psum)
        W("scalar", s_t)
        def evrow(scalar):
            scalar.copy(lnrow[:], ps_row[:]).then_inc(s_a.h, 1)
        s_a.v += 1
        E("scalar", evrow)
        W("tensor", s_a)
        def mm_bc(tensor):
            tensor.matmul(ps_bc[:], ones_r[:], lnrow[:], start=True, stop=True).then_inc(s_t.h, 1)
        s_t.v += 1
        E("tensor", mm_bc)
        W("scalar", s_t)
        def evbc(scalar):
            scalar.activation(rowb[:], ps_bc[:], AF.Copy, scale=1.0 / DI).then_inc(s_a.h, 1)
        s_a.v += 1
        E("scalar", evbc)
        W("vector", s_a)
        def ysub(vector):
            y3 = yb[:].rearrange("p (g t) -> p g t", g=G)
            mb_v = rowb[:].unsqueeze(1).broadcast_to((128, G, 256))
            vector.tensor_tensor(y3, y3, mb_v, ALU.subtract).then_inc(s_v.h, 1)
        s_v.v += 1
        E("vector", ysub)
        W("scalar", s_v)
        def ysq(scalar):
            scalar.activation(scr16[:], yb[:], AF.Square).then_inc(s_a.h, 1)
        s_a.v += 1
        E("scalar", ysq)
        W("vector", s_a)
        def red2(vector):
            q3 = scr16[:].rearrange("p (g t) -> p g t", g=G).transpose([0, 2, 1])
            vector.tensor_reduce(rowb[:], q3, AX.X, ALU.add).then_inc(s_v.h, 1)
        s_v.v += 1
        E("vector", red2)
        W("tensor", s_v)
        def mm_psum2(tensor):
            tensor.matmul(ps_row[:], ones_c[:], rowb[:], start=True, stop=True).then_inc(s_t.h, 1)
        s_t.v += 1
        E("tensor", mm_psum2)
        W("scalar", s_t)
        def evrow2(scalar, _w=s_a.v + 1):
            scalar.activation(lnrow[:], ps_row[:], AF.Ln, scale=1.0 / DI, bias=1e-5).then_inc(s_a.h, 1)
            scalar.wait_ge(s_a.h, _w)
            scalar.activation(lnrow[:], lnrow[:], AF.Exp, scale=-0.5).then_inc(s_a.h, 1)
        s_a.v += 2
        E("scalar", evrow2)
        W("tensor", s_a)
        def mm_bc2(tensor):
            tensor.matmul(ps_bc[:], ones_r[:], lnrow[:], start=True, stop=True).then_inc(s_t.h, 1)
        s_t.v += 1
        E("tensor", mm_bc2)
        W("scalar", s_t)
        def evbc2(scalar):
            scalar.copy(rowb[:], ps_bc[:]).then_inc(s_a.h, 1)
        s_a.v += 1
        E("scalar", evbc2)
        W("vector", s_a)
        def ygate(vector):
            y3 = yb[:].rearrange("p (g t) -> p g t", g=G)
            u3 = ub[:].rearrange("p (g t) -> p g t", g=G)
            t3 = scr16[:].rearrange("p (g t) -> p g t", g=G)
            rb_v = rowb[:].unsqueeze(1).broadcast_to((128, G, 256))
            lng_v = lng[:].unsqueeze(2).broadcast_to((128, G, 256))
            lnb_v = lnb[:].unsqueeze(2).broadcast_to((128, G, 256))
            dp_v = dpar[:].unsqueeze(2).broadcast_to((128, G, 256))
            vector.tensor_tensor(y3, y3, rb_v, ALU.mult)
            vector.tensor_tensor(y3, y3, lng_v, ALU.mult)
            vector.tensor_tensor(y3, y3, lnb_v, ALU.add)
            vector.tensor_tensor(t3, u3, dp_v, ALU.mult)
            vector.tensor_tensor(yb[:], yb[:], scr16[:], ALU.add)
            vector.tensor_tensor(yb[:], yb[:], zsb[:], ALU.mult)
            vector.tensor_copy(ygbf[:], yb[:]).then_inc(s_v.h, 1)
        s_v.v += 1
        E("vector", ygate)
        v_yg = s_v.v

        for ci in range(2):
            tt = cc * 2 + ci
            if n_out > 0:
                W("sync", s_v)
            def ldfin(sync, tt=tt):
                sync.dma_start(out=g2t[:], in_=g2_scr[tt * 128:(tt + 1) * 128, :]).then_inc(s_fin.h, 16)
                sync.dma_start(out=xt2[:], in_=x_in[tt * 128:(tt + 1) * 128, :]).then_inc(s_fin.h, 16)
            s_fin.v += 32
            E("sync", ldfin)
            for ec in range(G):
                if ec >= 2 or tt > 0:
                    W("sync", s_t, -1 if ec >= 2 else 0)
                def ldwo(sync, ec=ec):
                    sync.dma_start(out=woutb[ec % 2][:],
                                   in_=woutT_scr[ec * 128:(ec + 1) * 128, :]).then_inc(s_wo[ec % 2].h, 16)
                s_wo[ec % 2].v += 16
                E("sync", ldwo)
                W("tensor", s_wo[ec % 2])
                if ec == 0:
                    W("tensor", s_v, v_yg - s_v.v)
                    if tt > 0:
                        W("tensor", s_a)
                def womm(tensor, ec=ec, ci=ci):
                    ins = None
                    for dh in range(2):
                        ins = tensor.matmul(ps_s[:, dh * 512:(dh + 1) * 512],
                                            ygbf[:, ec * 256 + ci * 128:ec * 256 + ci * 128 + 128],
                                            woutb[ec % 2][:, dh * 512:(dh + 1) * 512],
                                            start=(ec == 0), stop=(ec == G - 1))
                    ins.then_inc(s_t.h, 1)
                s_t.v += 1
                E("tensor", womm)
            W("scalar", s_t)
            if n_out >= 1:
                W("scalar", s_sc)
            def sevac(scalar):
                scalar.copy(stile[:], ps_s[:]).then_inc(s_a.h, 1)
            s_a.v += 1
            E("scalar", sevac)
            W("vector", s_a)
            W("vector", s_fin)
            def fin(vector):
                vector.tensor_tensor(stile[:], stile[:], g2t[:], ALU.mult)
                vector.tensor_tensor(stile[:], stile[:], xt2[:], ALU.add).then_inc(s_v.h, 1)
            s_v.v += 1
            E("vector", fin)
            W("scalar", s_v)
            def stout(scalar, tt=tt):
                scalar.dma_start(out=y_out[tt * 128:(tt + 1) * 128, :], in_=stile[:]).then_inc(s_sc.h, 16)
            s_sc.v += 16
            E("scalar", stout)
            n_out += 1

    W("scalar", s_sc)
    W("sync", s_ld)
    run_block(SCH, "p2")
    es2.close()
    es_root.close()
    return nc


def _host_layout(inputs):
    A = np.ascontiguousarray
    gp = lambda a: np.asarray(a, dtype=np.float32)
    x = A(gp(inputs["L0_attn_x"]))
    d_cols = lambda v: A(gp(v).reshape(G, 128).T)
    maps = {
        "w_inT": A(gp(inputs["L0_attn_W_in"]).T),
        "conv_w": A(gp(inputs["L0_attn_conv_w"]).reshape(DI, 4).reshape(G, 128, 4)
                    .transpose(1, 0, 2).reshape(128, G * 4)),
        "conv_b": d_cols(inputs["L0_attn_conv_b"]),
        "w_xT": A(gp(inputs["L0_attn_W_x"]).T.reshape(G, 128, 33)
                  .transpose(1, 0, 2).reshape(128, G * 33)),
        "w_dt": d_cols(gp(inputs["L0_attn_W_dt"]).reshape(DI)),
        "b_dt": d_cols(inputs["L0_attn_b_dt"]),
        "a_log": gp(inputs["L0_attn_A_log"]).reshape(1, NS),
        "d_par": d_cols(inputs["L0_attn_D_param"]),
        "w_outT": A(gp(inputs["L0_attn_W_out"]).T),
        "ln_g": d_cols(inputs["L0_attn_ln_g"]),
        "ln_b": d_cols(inputs["L0_attn_ln_b"]),
        "gate_wT": A(gp(inputs["L0_ffn_gate_w"]).T),
        "w1_all": A(np.concatenate([gp(inputs["L0_ffn_W1"]),
                                    gp(inputs["L0_ffn_sW1"])[None]], axis=0)),
        "b1_all": A(np.concatenate([gp(inputs["L0_ffn_b1"]),
                                    gp(inputs["L0_ffn_sb1"])[None]], axis=0)
                    .reshape(NE, GF, 128).transpose(2, 0, 1).reshape(128, NE * GF)),
        "w2_all": A(np.concatenate([gp(inputs["L0_ffn_W2"]),
                                    gp(inputs["L0_ffn_sW2"])[None]], axis=0)),
        "b2_ext": A(np.concatenate([gp(inputs["L0_ffn_b2"]),
                                    gp(inputs["L0_ffn_sb2"])[None]], axis=0)),
    }
    return x, maps


def kernel(**inputs):
    x, wmaps = _host_layout(inputs)
    if "nc" not in _CACHE:
        _CACHE["nc"] = build_program()
    nc = _CACHE["nc"]
    in_maps = [dict(wmaps, x_in=np.ascontiguousarray(x[i])) for i in range(8)]
    res = run_bass_kernel_spmd(nc, in_maps, core_ids=list(range(8)))
    return np.stack([r["y_out"] for r in res.results], axis=0)
